# version 06 (final)
# baseline (speedup 1.0000x reference)
import contextlib
import numpy as np
import ml_dtypes
import concourse.bass as bass
import concourse.mybir as mybir
from concourse.bass_utils import run_bass_kernel_spmd

F32 = mybir.dt.float32
BF16 = mybir.dt.bfloat16
AF = mybir.ActivationFunctionType
ALU = mybir.AluOpType
AX = mybir.AxisListType

NCORES = 8
B_PER_CORE = 2
S = 2048
D = 1024
EPS = 1e-5


class Ch:

    ROT = 20000

    def __init__(self):
        self.ops = []

    def op(self, eng, fn, inc=1):
        self.ops.append((eng, fn, inc))

    def dma(self, out, in_, **kw):
        self.op("sync", lambda e: e.dma_start(out=out, in_=in_, **kw), 16)

    def emit(self, nc, stack):
        waits = []
        incs = []
        cum = base = cs = 0
        gcum = gbase = gcs = 0
        for (eng, _fn, inc) in self.ops:
            w = []
            rel = cum - base
            if rel >= self.ROT:
                w.append(("m", cs, rel))
                cs += 1
                base = cum
                rel = 0
            grel = gcum - gbase
            if grel >= self.ROT:
                w.append(("g", gcs, grel))
                gcs += 1
                gbase = gcum
                grel = 0
            if rel > 0:
                w.append(("m", cs, rel))
            if grel > 0:
                w.append(("g", gcs, grel))
            waits.append(w)
            if eng == "gpsimd":
                incs.append(("g", gcs, inc))
                gcum += inc
            else:
                incs.append(("m", cs, inc))
                cum += inc
        msems = [stack.enter_context(nc.semaphore(f"chain{i}"))
                 for i in range(cs + 1)]
        gsems = [stack.enter_context(nc.semaphore(f"gchain{i}"))
                 for i in range(gcs + 1)]
        sems = {"m": msems, "g": gsems}
        finals = [("m", cs, cum - base)]
        if gcum - gbase > 0 or gcs > 0:
            finals.append(("g", gcs, gcum - gbase))
        block = stack.enter_context(nc.Block())
        engs = set(e for (e, _, _) in self.ops)

        def section(eng_name):
            def body(engobj):
                for i, (e, fn, _inc) in enumerate(self.ops):
                    if e != eng_name:
                        continue
                    for (fam, s_i, v) in waits[i]:
                        engobj.wait_ge(sems[fam][s_i], v)
                    instr = fn(engobj)
                    fam, s_i, amt = incs[i]
                    instr.then_inc(sems[fam][s_i], amt)
                for (fam, s_i, v) in finals:
                    engobj.wait_ge(sems[fam][s_i], v)
            return body

        for name in ("sync", "tensor", "vector", "scalar", "gpsimd"):
            if name in engs or name == "sync":
                dec = getattr(block, name)
                dec(section(name))


def ceil_div(a, b):
    return -(-a // b)


def chunks(total, step):
    out = []
    o = 0
    while o < total:
        out.append((o, min(step, total - o)))
        o += step
    return out


class Builder:

    def __init__(self, nc, stack, ch):
        self.nc = nc
        self.stack = stack
        self.ch = ch
        self._uid = 0
        self._consts = {}

    def constf(self, *vals):
        for v in vals:
            v = float(v)
            if v in self._consts:
                continue
            t = self.sb("cst", [128, 1], F32)
            self.v(lambda e, t=t, v=v: e.memset(t[:, 0:1], v))
            self.nc.const_aps.register(F32, v, t[:, 0:1])
            self._consts[v] = t

    def uid(self):
        self._uid += 1
        return self._uid

    def sb(self, name, shape, dtype):
        return self.stack.enter_context(
            self.nc.sbuf_tensor(f"{name}_{self.uid()}", shape, dtype)
        )

    def sb_in(self, stack, name, shape, dtype):
        return stack.enter_context(
            self.nc.sbuf_tensor(f"{name}_{self.uid()}", shape, dtype)
        )

    def ps(self, name, shape, dtype=F32):
        return self.stack.enter_context(
            self.nc.psum_tensor(f"{name}_{self.uid()}", shape, dtype)
        )

    def v(self, fn):
        self.ch.op("vector", fn)

    def s(self, fn):
        self.ch.op("scalar", fn)

    def t(self, fn, inc=1):
        self.ch.op("tensor", fn, inc)

    def dma(self, out, in_, **kw):
        self.ch.dma(out, in_, **kw)

    def dmac(self, out, in_, **kw):
        self.ch.op("gpsimd", lambda e: e.dma_start(out=out, in_=in_, **kw), 16)

    def mm_group(self, out_ap, pairs):
        n = len(pairs)

        def fn(e):
            last = None
            for i, (l, r) in enumerate(pairs):
                last = e.matmul(out_ap, l, r, start=(i == 0), stop=(i == n - 1))
            return last

        self.t(fn)

    def transpose(self, out_psum, in_sb, ident):
        self.t(lambda e: e.transpose(out_psum, in_sb, ident))

    def load_cast_rows(self, dram_ap, rows, cols, stage_f32, out_tiles):
        for i, (r0, rn) in enumerate(chunks(rows, 128)):
            self.dma(stage_f32[0:rn, 0:cols], dram_ap[r0 : r0 + rn, 0:cols])
            ot = out_tiles[i]
            self.s(lambda e, ot=ot, rn=rn: e.copy(ot[0:rn, 0:cols], stage_f32[0:rn, 0:cols]))

    def load_transposed(self, dram_ap, rows, cols, stage_f32, stage_bf, out_tiles,
                        psT, ident):
        for (r0, _rn) in chunks(rows, 128):
            self.dma(stage_f32[:, 0:cols], dram_ap[r0 : r0 + 128, 0:cols])
            self.s(lambda e: e.copy(stage_bf[:, 0:cols], stage_f32[:, 0:cols]))
            for j, (c0, _cn) in enumerate(chunks(cols, 128)):
                self.transpose(psT[:, :], stage_bf[:, c0 : c0 + 128], ident)
                ot = out_tiles[j]
                self.s(lambda e, ot=ot, r0=r0: e.copy(ot[:, r0 : r0 + 128], psT[:, :]))

    def col_from_dram(self, dram_1d, off, n, tile):
        self.dma(tile[0:n, 0:1], dram_1d[off : off + n].rearrange("(p one) -> p one", one=1))

    def token_ln(self, x_ap, out_ap, dump, mcol, vcol, sdcol, rscol, xc, n_feat,
                 eps=EPS, g_col=None, b_col=None):
        self.s(lambda e: e.activation(dump[:, 0:n_feat], x_ap, AF.Identity,
                                      scale=1.0 / n_feat, accum_out=mcol[:, 0:1]))
        self.v(lambda e: e.tensor_scalar(xc[:, 0:n_feat], x_ap, mcol[:, 0:1], None,
                                         ALU.subtract))
        self.s(lambda e: e.activation(dump[:, 0:n_feat], xc[:, 0:n_feat], AF.Square,
                                      accum_out=vcol[:, 0:1]))
        self.s(lambda e: e.activation(sdcol[:, 0:1], vcol[:, 0:1], AF.Sqrt,
                                      bias=float(eps), scale=1.0 / n_feat))
        self.v(lambda e: e.reciprocal(rscol[:, 0:1], sdcol[:, 0:1]))
        if g_col is None:
            self.v(lambda e: e.tensor_scalar(out_ap, xc[:, 0:n_feat], rscol[:, 0:1],
                                             None, ALU.mult))
        else:
            self.v(lambda e: e.tensor_scalar(xc[:, 0:n_feat], xc[:, 0:n_feat],
                                             rscol[:, 0:1], None, ALU.mult))
            self.v(lambda e: e.scalar_tensor_tensor(out_ap, xc[:, 0:n_feat],
                                                    g_col[:, 0:1], b_col,
                                                    ALU.mult, ALU.add))


def build_A():
    nc = bass.Bass("TRN2", target_bir_lowering=False)
    stack = contextlib.ExitStack()
    ch = Ch()

    xin = nc.dram_tensor("xin", [B_PER_CORE, S, D], F32, kind="ExternalInput").ap()
    wq = nc.dram_tensor("wq", [D, D], F32, kind="ExternalInput").ap()
    bq = nc.dram_tensor("bq", [D], F32, kind="ExternalInput").ap()
    wv = nc.dram_tensor("wv", [D, D], F32, kind="ExternalInput").ap()
    bv = nc.dram_tensor("bv", [D], F32, kind="ExternalInput").ap()
    anom = nc.dram_tensor("anom", [D, D], F32, kind="ExternalInput").ap()
    identd = nc.dram_tensor("ident", [128, 128], BF16, kind="ExternalInput").ap()
    identfd = nc.dram_tensor("identf", [128, 128], F32, kind="ExternalInput").ap()
    xout = nc.dram_tensor("xout", [B_PER_CORE, D, S], F32, kind="ExternalOutput").ap()

    b = Builder(nc, stack, ch)
    b.constf(0.0, 1.0, EPS, 1.0 / D, 1.0 / 32.0, 1e-9)

    NT = S // 128
    ident_t = b.sb("ident", [128, 128], BF16)
    b.dma(ident_t[:, :], identd)
    ident = ident_t[:, :]
    identf_t = b.sb("identf", [128, 128], F32)
    b.dma(identf_t[:, :], identfd)
    identf = identf_t[:, :]
    psTf = b.ps("psTf", [128, 128], F32)
    tout = b.sb("tout", [128, 128], F32)

    stage = b.sb("stage", [128, D], F32)
    stageb = b.sb("stageb", [128, D], BF16)
    psT = b.ps("psT", [128, 128], BF16)
    ps0 = b.ps("ps0", [128, 512], F32)
    ps1 = b.ps("ps1", [128, 512], F32)
    ps2 = b.ps("ps2", [128, 512], F32)

    WqT = [b.sb(f"WqT{j}", [128, D], BF16) for j in range(8)]
    WvT = [b.sb(f"WvT{j}", [128, D], BF16) for j in range(8)]
    AN = [b.sb(f"AN{j}", [128, D], BF16) for j in range(8)]
    b.load_transposed(wq, D, D, stage, stageb, WqT, psT, ident)
    b.load_transposed(wv, D, D, stage, stageb, WvT, psT, ident)
    b.load_cast_rows(anom, D, D, stage, AN)

    bq_col = [b.sb(f"bqc{j}", [128, 1], F32) for j in range(8)]
    for j in range(8):
        b.col_from_dram(bq, j * 128, 128, bq_col[j])
    bv_row = b.sb("bvrow", [1, D], F32)
    b.dma(bv_row[0:1, 0:D], bv.rearrange("(one d) -> one d", one=1))
    ones_row = b.sb("ones_row", [1, 128], F32)
    self_memset = lambda e: e.memset(ones_row[:, :], 1.0)
    ch.op("vector", self_memset)
    BVB = b.sb("BVB", [128, D], F32)
    for f in range(2):
        b.mm_group(ps0[:, 0:512], [(ones_row[0:1, 0:128], bv_row[0:1, f * 512:(f + 1) * 512])])
        b.s(lambda e, f=f: e.copy(BVB[:, f * 512:(f + 1) * 512], ps0[:, 0:512]))

    QT = [b.sb(f"QT{j}", [128, S], BF16) for j in range(8)]
    V = [b.sb(f"V{t}", [128, D], BF16) for t in range(NT)]
    ST = [b.sb(f"ST{j}", [128, S], BF16) for j in range(8)]
    TQ = ST
    S_sb = b.sb("S_sb", [128, S], F32)
    PB = [b.sb("PB", [128, S], BF16)] * 4
    PT = [b.sb(f"PT{j}", [128, 512], BF16) for j in range(NT)]
    xtm = b.sb("xtm", [128, D], F32)
    dump = b.sb("dump", [128, D], F32)
    xc = b.sb("xc", [128, D], F32)
    otn = xtm
    ssq = b.sb("ssq", [128, NT], F32)
    norm_col = b.sb("normc", [128, NT], F32)
    inv_col = b.sb("invc", [128, NT], F32)
    mcol = b.sb("mcol", [128, 1], F32)
    vcol = b.sb("vcol", [128, 1], F32)
    sdcol = b.sb("sdcol", [128, 1], F32)
    rscol = b.sb("rscol", [128, 1], F32)
    smax = b.sb("smax", [128, 1], F32)
    rsum = b.sb("rsum", [128, 1], F32)
    rinv_cols = b.sb("rinvc", [128, NT], F32)
    rn_col = b.sb("rncol", [128, 1], F32)
    xbn = b.sb("xbn", [128, D], F32)

    for bb in range(B_PER_CORE):
        for tc in range(NT):
            b.dma(xtm[:, :], xin[bb, tc * 128:(tc + 1) * 128, :])
            b.s(lambda e, tc=tc: e.activation(dump[:, :], xtm[:, :], AF.Square,
                                              accum_out=ssq[:, tc:tc + 1]))
        b.s(lambda e: e.activation(norm_col[:, 0:NT], ssq[:, 0:NT], AF.Sqrt))
        b.v(lambda e: e.tensor_scalar(inv_col[:, 0:NT], norm_col[:, 0:NT], 1e-9, None,
                                      ALU.add))
        b.v(lambda e: e.reciprocal(inv_col[:, 0:NT], inv_col[:, 0:NT]))
        for tc in range(NT):
            b.dma(xtm[:, :], xin[bb, tc * 128:(tc + 1) * 128, :])
            b.v(lambda e, tc=tc: e.tensor_scalar(stageb[:, 0:D], xtm[:, :],
                                                 inv_col[:, tc:tc + 1], None, ALU.mult))
            for j in range(8):
                b.transpose(psT[:, :], stageb[:, j * 128:(j + 1) * 128], ident)
                b.s(lambda e, j=j, tc=tc: e.copy(ST[j][:, tc * 128:(tc + 1) * 128],
                                                 psT[:, :]))
        for po in range(8):
            for f in range(4):
                b.mm_group(ps0[:, 0:512],
                           [(WqT[c][:, po * 128:(po + 1) * 128],
                             ST[c][:, f * 512:(f + 1) * 512]) for c in range(8)])
                b.s(lambda e, po=po, f=f: e.activation(
                    QT[po][:, f * 512:(f + 1) * 512], ps0[:, 0:512], AF.Identity,
                    bias=bq_col[po][:, 0:1]))
        for tc in range(NT):
            for f in range(2):
                b.mm_group(ps1[:, 0:512],
                           [(ST[c][:, tc * 128:(tc + 1) * 128],
                             WvT[c][:, f * 512:(f + 1) * 512]) for c in range(8)])
                b.s(lambda e, tc=tc, f=f: e.copy(V[tc][:, f * 512:(f + 1) * 512],
                                                 ps1[:, 0:512]))
        for po in range(8):
            for f in range(4):
                b.mm_group(ps0[:, 0:512],
                           [(AN[c][:, po * 128:(po + 1) * 128],
                             QT[c][:, f * 512:(f + 1) * 512]) for c in range(8)])
                b.s(lambda e, po=po, f=f: e.copy(TQ[po][:, f * 512:(f + 1) * 512],
                                                 ps0[:, 0:512]))
        for sp in range(4):
            for ii in range(4):
                ic = sp * 4 + ii
                for jc in range(4):
                    b.mm_group(ps0[:, 0:512],
                               [(QT[c][:, ic * 128:(ic + 1) * 128],
                                 TQ[c][:, jc * 512:(jc + 1) * 512]) for c in range(8)])
                    b.s(lambda e, jc=jc: e.activation(
                        S_sb[:, jc * 512:(jc + 1) * 512], ps0[:, 0:512], AF.Identity,
                        scale=1.0 / 32.0))
                b.v(lambda e: e.tensor_reduce(smax[:, 0:1], S_sb[:, 0:S], AX.X,
                                              ALU.max, negate=True))
                b.s(lambda e, ii=ii: e.activation(PB[ii][:, 0:S], S_sb[:, 0:S], AF.Exp,
                                                  bias=smax[:, 0:1],
                                                  accum_out=rsum[:, 0:1]))
                b.v(lambda e, ic=ic: e.reciprocal(rinv_cols[:, ic:ic + 1],
                                                  rsum[:, 0:1]))
                for jc in range(NT):
                    b.transpose(psT[:, :], PB[ii][:, jc * 128:(jc + 1) * 128], ident)
                    b.s(lambda e, jc=jc, ii=ii: e.copy(PT[jc][:, ii * 128:(ii + 1) * 128],
                                                       psT[:, :]))
            for ii in range(4):
                ic = sp * 4 + ii
                b.dma(xtm[:, :], xin[bb, ic * 128:(ic + 1) * 128, :])
                b.v(lambda e, ic=ic: e.tensor_mul(rn_col[:, 0:1],
                                                  rinv_cols[:, ic:ic + 1],
                                                  norm_col[:, ic:ic + 1]))
                b.v(lambda e, ic=ic: e.scalar_tensor_tensor(
                    xbn[:, 0:D], BVB[:, 0:D], norm_col[:, ic:ic + 1], xtm[:, 0:D],
                    ALU.mult, ALU.add))
                for f in range(2):
                    b.mm_group(ps2[:, 0:512],
                               [(PT[jc][:, ii * 128:(ii + 1) * 128],
                                 V[jc][:, f * 512:(f + 1) * 512]) for jc in range(NT)])
                    b.v(lambda e, f=f: e.scalar_tensor_tensor(
                        dump[:, f * 512:(f + 1) * 512], ps2[:, 0:512],
                        rn_col[:, 0:1], xbn[:, f * 512:(f + 1) * 512],
                        ALU.mult, ALU.add))
                b.token_ln(dump[:, 0:D], otn[:, 0:D], xc, mcol, vcol, sdcol, rscol,
                           stage, D)
                for j in range(8):
                    b.transpose(psTf[:, :], otn[:, j * 128:(j + 1) * 128], identf)
                    b.s(lambda e: e.copy(tout[:, :], psTf[:, :]))
                    b.dma(xout[bb, j * 128:(j + 1) * 128, ic * 128:(ic + 1) * 128],
                          tout[:, :])

    ch.emit(nc, stack)
    stack.close()
    return nc




class LayerCfg:
    def __init__(self, heads, window, postln):
        self.H = heads
        self.W = window
        self.INNER = heads * 64
        self.QKV = 3 * self.INNER
        self.PO_Q = self.INNER // 128
        self.postln = postln
        self.HID = 2730
        self.to_token_major = False


def build_layer(cfg):
    nc = bass.Bass("TRN2", target_bir_lowering=False)
    stack = contextlib.ExitStack()
    astack = contextlib.ExitStack()
    ch = Ch()
    b = Builder(nc, stack, ch)

    W = cfg.W
    INNER = cfg.INNER
    TSP = 1024
    NSP = S // TSP
    KL = TSP + W
    WN = TSP // W
    HID = cfg.HID

    xin = nc.dram_tensor("xin", [B_PER_CORE, D, S], F32, kind="ExternalInput").ap()
    xmid = nc.dram_tensor("xmid", [B_PER_CORE, D, S], F32, kind="Internal").ap()
    out_shape = [B_PER_CORE, S, D] if cfg.to_token_major else [B_PER_CORE, D, S]
    xout = nc.dram_tensor("xout", out_shape, F32, kind="ExternalOutput").ap()
    if cfg.postln:
        xblk = nc.dram_tensor("xblk", [B_PER_CORE, D, S], F32,
                              kind="ExternalInput").ap()
    identfd = nc.dram_tensor("identf", [128, 128], F32, kind="ExternalInput").ap()
    wqkv = nc.dram_tensor("wqkv", [D, cfg.QKV], F32, kind="ExternalInput").ap()
    qs_d = nc.dram_tensor("qs", [64], F32, kind="ExternalInput").ap()
    ks_d = nc.dram_tensor("ks", [64], F32, kind="ExternalInput").ap()
    wo_d = nc.dram_tensor("wo", [INNER, D], F32, kind="ExternalInput").ap()
    lng = nc.dram_tensor("lng", [D], F32, kind="ExternalInput").ap()
    lnb = nc.dram_tensor("lnb", [D], F32, kind="ExternalInput").ap()
    w1_d = nc.dram_tensor("w1", [D, 2 * HID], F32, kind="ExternalInput").ap()
    b1_d = nc.dram_tensor("b1", [2 * HID], F32, kind="ExternalInput").ap()
    w2_d = nc.dram_tensor("w2", [HID, D], F32, kind="ExternalInput").ap()
    b2_d = nc.dram_tensor("b2", [D], F32, kind="ExternalInput").ap()
    identd = nc.dram_tensor("ident", [128, 128], BF16, kind="ExternalInput").ap()
    rm_d = nc.dram_tensor("rm", [128, 128], F32, kind="ExternalInput").ap()
    csq_d = nc.dram_tensor("csq", [128, S], F32, kind="ExternalInput").ap()
    ssq_d = nc.dram_tensor("ssq", [128, S], F32, kind="ExternalInput").ap()
    csk_d = nc.dram_tensor("csk", [128, S], F32, kind="ExternalInput").ap()
    ssk_d = nc.dram_tensor("ssk", [128, S], F32, kind="ExternalInput").ap()
    m1_d = nc.dram_tensor("m1", [W, 2 * W], F32, kind="ExternalInput").ap()
    m0_d = nc.dram_tensor("m0", [W, W], F32, kind="ExternalInput").ap()
    sel64_d = nc.dram_tensor("sel64", [128, 2], F32, kind="ExternalInput").ap()
    selt_d = nc.dram_tensor("selt", [2, 128], F32, kind="ExternalInput").ap()

    b.constf(0.0, 1.0, EPS, 1.0 / D, 1e-12, 8.0)

    ident_t = b.sb("ident", [128, 128], BF16)
    b.dma(ident_t[:, :], identd)
    ident = ident_t[:, :]
    if cfg.to_token_major:
        identf_t = b.sb("identf", [128, 128], F32)
        b.dma(identf_t[:, :], identfd)
        identf = identf_t[:, :]
        tout = b.sb("tout", [128, 128], F32)
    ones_col = b.sb("ones_col", [128, 1], F32)
    b.v(lambda e: e.memset(ones_col[:, 0:1], 1.0))
    ones_row = b.sb("ones_row", [1, 128], F32)
    b.v(lambda e: e.memset(ones_row[0:1, :], 1.0))
    g_col = [b.sb(f"gcol{j}", [128, 1], F32) for j in range(8)]
    bc_col = [b.sb(f"bcol{j}", [128, 1], F32) for j in range(8)]
    b2c = [b.sb(f"b2c{j}", [128, 1], F32) for j in range(8)]
    for j in range(8):
        b.col_from_dram(lng, j * 128, 128, g_col[j])
        b.col_from_dram(lnb, j * 128, 128, bc_col[j])
        b.col_from_dram(b2_d, j * 128, 128, b2c[j])
    XF = [b.sb(f"XF{j}", [128, TSP], F32) for j in range(8)]
    dumpA = b.sb("dumpA", [128, 512], F32)

    psQK = b.ps("psQK", [128, 512], F32)
    psS = b.ps("psS", [128, 1024], F32)
    psN = b.ps("psN", [2, 512], F32)
    psB = b.ps("psB", [128, 512], F32)
    psR = b.ps("psR", [128, 512], F32)
    psTb = b.ps("psTb", [128, 128], BF16)
    psO = b.ps("psO", [128, 512], F32)

    RM = b.sb_in(astack, "rm", [128, 128], F32)
    b.dma(RM[:, :], rm_d)
    sel64 = b.sb_in(astack, "sel64", [128, 2], F32)
    b.dma(sel64[:, 0:2], sel64_d)
    selT = b.sb_in(astack, "selT", [2, 128], F32)
    b.dma(selT[0:2, 0:128], selt_d)
    qs8 = b.sb_in(astack, "qs8", [128, 1], F32)
    ksc = b.sb_in(astack, "ksc", [128, 1], F32)
    for half in range(2):
        b.dma(qs8[half * 64:(half + 1) * 64, 0:1],
              qs_d[0:64].rearrange("(p one) -> p one", one=1))
        b.dma(ksc[half * 64:(half + 1) * 64, 0:1],
              ks_d[0:64].rearrange("(p one) -> p one", one=1))
    b.v(lambda e: e.tensor_scalar(qs8[:, 0:1], qs8[:, 0:1], 8.0, None, ALU.mult))

    NQC = W // 128
    M1 = [b.sb_in(astack, f"m1_{i}", [128, 2 * W], F32) for i in range(NQC)]
    M0 = [b.sb_in(astack, f"m0_{i}", [128, W], F32) for i in range(NQC)]
    for i in range(NQC):
        b.dma(M1[i][:, :], m1_d[i * 128:(i + 1) * 128, :])
        b.dma(M0[i][:, :], m0_d[i * 128:(i + 1) * 128, :])

    WQKV = [b.sb_in(astack, f"WQKV{c}", [128, cfg.QKV], BF16) for c in range(8)]
    for c in range(8):
        b.dmac(WQKV[c][:, :], wqkv[c * 128:(c + 1) * 128, :])
    XS = [b.sb_in(astack, f"XS{j}", [128, KL], BF16) for j in range(8)]
    AT = [XS[j][:, 0:TSP] for j in range(cfg.INNER // 128)]
    QTB = [b.sb_in(astack, f"QTB{j}", [128, TSP], BF16) for j in range(cfg.PO_Q)]
    WO = QTB
    KTB = [b.sb_in(astack, f"KTB{j}", [128, KL], BF16) for j in range(cfg.PO_Q)]
    V = [b.sb_in(astack, f"V{t}", [128, INNER], BF16) for t in range(KL // 128)]
    TCS = b.sb_in(astack, "TCS", [128, KL], F32)
    TSS = b.sb_in(astack, "TSS", [128, KL], F32)
    S_sb = b.sb_in(astack, "S_sb", [128, 2 * W], F32)
    PBr = b.sb_in(astack, "PBr", [128, 2 * W], BF16)
    PBn = b.sb_in(astack, "PBn", [128, 2 * W], BF16)
    PTW = [b.sb_in(astack, f"PTW{j}", [128, W], BF16) for j in range(2 * W // 128)]
    QN = b.sb_in(astack, "QN", [128, 512], F32)
    M1T = b.sb_in(astack, "M1T", [128, 512], F32)
    T1 = b.sb_in(astack, "T1", [128, 512], F32)
    RB = b.sb_in(astack, "RB", [128, 512], F32)
    nrow = b.sb_in(astack, "nrow", [2, 512], F32)
    nrow2 = b.sb_in(astack, "nrow2", [2, 512], F32)
    smax = b.sb_in(astack, "smax", [128, 1], F32)
    rsum = b.sb_in(astack, "rsum", [128, 1], F32)
    rinv = b.sb_in(astack, "rinv", [128, 1], F32)

    def normrot(psrc, wd, scale_col, cs_ap, ss_ap, dst_ap):
        b.s(lambda e: e.activation(dumpA[:, 0:wd], psrc, AF.Square))
        b.t(lambda e: e.matmul(psN[0:2, 0:wd], sel64[:, 0:2], dumpA[:, 0:wd],
                               start=True, stop=True))
        b.s(lambda e: e.activation(nrow[0:2, 0:wd], psN[0:2, 0:wd], AF.Sqrt))
        b.v(lambda e: e.tensor_scalar(nrow2[0:2, 0:wd], nrow[0:2, 0:wd], 1e-12,
                                      None, ALU.max))
        b.v(lambda e: e.reciprocal(nrow2[0:2, 0:wd], nrow2[0:2, 0:wd]))
        b.t(lambda e: e.matmul(psB[:, 0:wd], selT[0:2, :], nrow2[0:2, 0:wd],
                               start=True, stop=True))
        b.s(lambda e: e.copy(RB[:, 0:wd], psB[:, 0:wd]))
        b.v(lambda e: e.scalar_tensor_tensor(QN[:, 0:wd], psrc, scale_col[:, 0:1],
                                             RB[:, 0:wd], ALU.mult, ALU.mult))
        b.t(lambda e: e.matmul(psR[:, 0:wd], RM[:, :], QN[:, 0:wd],
                               start=True, stop=True))
        b.v(lambda e: e.tensor_mul(M1T[:, 0:wd], QN[:, 0:wd], cs_ap))
        b.v(lambda e: e.tensor_mul(T1[:, 0:wd], psR[:, 0:wd], ss_ap))
        b.v(lambda e: e.tensor_add(dst_ap, T1[:, 0:wd], M1T[:, 0:wd]))

    for bb in range(B_PER_CORE):
        for sp in range(NSP):
            t0 = sp * TSP
            for j in range(8):
                b.dma(XF[j][:, :], xin[bb, j * 128:(j + 1) * 128, t0:t0 + TSP])
            for j in range(8):
                sh = 1 if j >= 4 else 0
                if sp > 0:
                    b.dmac(XS[j][:, 0:KL],
                           xin[bb, j * 128:(j + 1) * 128, t0 - W - sh:t0 + TSP - sh])
                else:
                    b.v(lambda e, j=j: e.memset(XS[j][:, 0:W + sh], 0.0))
                    b.dmac(XS[j][:, W + sh:KL],
                           xin[bb, j * 128:(j + 1) * 128, 0:TSP - sh])
            b.dma(TCS[:, 0:TSP], csq_d[:, t0:t0 + TSP])
            b.dma(TSS[:, 0:TSP], ssq_d[:, t0:t0 + TSP])
            for po in range(cfg.PO_Q):
                for f in range(TSP // 512):
                    b.mm_group(psQK[:, 0:512],
                               [(WQKV[c][:, po * 128:(po + 1) * 128],
                                 XS[c][:, W + f * 512:W + (f + 1) * 512])
                                for c in range(8)])
                    normrot(psQK[:, 0:512], 512, qs8,
                            TCS[:, f * 512:(f + 1) * 512],
                            TSS[:, f * 512:(f + 1) * 512],
                            QTB[po][:, f * 512:(f + 1) * 512])
            if sp > 0:
                b.dma(TCS[:, 0:KL], csk_d[:, t0 - W:t0 + TSP])
                b.dma(TSS[:, 0:KL], ssk_d[:, t0 - W:t0 + TSP])
            else:
                b.v(lambda e: e.memset(TCS[:, 0:W], 0.0))
                b.v(lambda e: e.memset(TSS[:, 0:W], 0.0))
                b.dma(TCS[:, W:KL], csk_d[:, 0:TSP])
                b.dma(TSS[:, W:KL], ssk_d[:, 0:TSP])
            for po in range(cfg.PO_Q):
                for (f0, fn) in chunks(KL, 512):
                    b.mm_group(psQK[:, 0:fn],
                               [(WQKV[c][:, INNER + po * 128:INNER + po * 128 + 128],
                                 XS[c][:, f0:f0 + fn]) for c in range(8)])
                    normrot(psQK[:, 0:fn], fn, ksc,
                            TCS[:, f0:f0 + fn], TSS[:, f0:f0 + fn],
                            KTB[po][:, f0:f0 + fn])
            for tc in range(KL // 128):
                for (f0, fn) in chunks(INNER, 512):
                    b.mm_group(psQK[:, 0:fn],
                               [(XS[c][:, tc * 128:(tc + 1) * 128],
                                 WQKV[c][:, 2 * INNER + f0:2 * INNER + f0 + fn])
                                for c in range(8)])
                    b.s(lambda e, tc=tc, f0=f0, fn=fn: e.copy(
                        V[tc][:, f0:f0 + fn], psQK[:, 0:fn]))
            for w in range(WN):
                first = (sp == 0 and w == 0)
                kc0 = w * W if not first else W
                kw = 2 * W if not first else W
                for h in range(cfg.H):
                    ph = h // 2
                    base = 64 * (h % 2)
                    for qc in range(NQC):
                        qcol = w * W + qc * 128
                        for (f0, fn) in chunks(kw, 512):
                            b.t(lambda e, ph=ph, base=base, qcol=qcol, kc0=kc0,
                                f0=f0, fn=fn: e.matmul(
                                    psS[:, f0:f0 + fn],
                                    QTB[ph][base:base + 64, qcol:qcol + 128],
                                    KTB[ph][base:base + 64, kc0 + f0:kc0 + f0 + fn],
                                    start=True, stop=True))
                        mt = M0[qc] if first else M1[qc]
                        b.v(lambda e, kw=kw, mt=mt: e.tensor_add(
                            S_sb[:, 0:kw], psS[:, 0:kw], mt[:, 0:kw]))
                        b.v(lambda e, kw=kw: e.tensor_reduce(
                            smax[:, 0:1], S_sb[:, 0:kw], AX.X, ALU.max, negate=True))
                        b.s(lambda e, kw=kw: e.activation(
                            PBr[:, 0:kw], S_sb[:, 0:kw], AF.Exp, bias=smax[:, 0:1],
                            accum_out=rsum[:, 0:1]))
                        b.v(lambda e: e.reciprocal(rinv[:, 0:1], rsum[:, 0:1]))
                        b.v(lambda e, kw=kw: e.tensor_scalar(
                            PBn[:, 0:kw], PBr[:, 0:kw], rinv[:, 0:1], None, ALU.mult))
                        for jc in range(kw // 128):
                            b.transpose(psTb[:, :], PBn[:, jc * 128:(jc + 1) * 128],
                                        ident)
                            b.s(lambda e, jc=jc, qc=qc: e.copy(
                                PTW[jc][:, qc * 128:(qc + 1) * 128], psTb[:, :]))
                    b.mm_group(psO[base:base + 64, 0:W],
                               [(V[kc0 // 128 + jc][:, h * 64:(h + 1) * 64],
                                 PTW[jc][:, 0:W]) for jc in range(kw // 128)])
                    b.s(lambda e, h=h, base=base, w=w: e.copy(
                        AT[h // 2][base:base + 64, w * W:(w + 1) * W],
                        psO[base:base + 64, 0:W]))
            for c in range(INNER // 128):
                b.dmac(WO[c][:, 0:D], wo_d[c * 128:(c + 1) * 128, :])
            for po in range(8):
                for f in range(2):
                    b.mm_group(psQK[:, 0:512],
                               [(WO[c][:, po * 128:(po + 1) * 128],
                                 AT[c][:, f * 512:(f + 1) * 512])
                                for c in range(INNER // 128)])
                    b.v(lambda e, po=po, f=f: e.tensor_add(
                        XF[po][:, f * 512:(f + 1) * 512], psQK[:, 0:512],
                        XF[po][:, f * 512:(f + 1) * 512]))
            for j in range(8):
                b.dma(xmid[bb, j * 128:(j + 1) * 128, t0:t0 + TSP], XF[j][:, :])

    astack.close()
    XSF = [b.sb(f"XSF{j}", [128, TSP], F32) for j in range(4)]
    XN = [b.sb(f"XN{j}", [128, TSP], BF16) for j in range(8)]
    SUMX = b.sb("SUMX", [1, 512], F32)
    SUMQ = b.sb("SUMQ", [1, 512], F32)
    M2 = b.sb("M2", [1, 512], F32)
    VRR = b.sb("VRR", [1, 512], F32)
    SDR = b.sb("SDR", [1, 512], F32)
    RSR = b.sb("RSR", [1, 512], F32)
    MRR = b.sb("MRR", [1, 512], F32)
    RSB = b.sb("RSB", [128, TSP], F32)
    MRB = b.sb("MRB", [128, TSP], F32)
    GA = b.sb("GA", [128, 512], F32)
    dump2 = b.sb("dump2", [128, 512], F32)
    a_chunks = chunks(HID, 128)
    W2R = [b.sb(f"W2R{c}", [128, D], BF16) for c in range(len(a_chunks))]
    H2 = [b.sb(f"H2_{c}", [128, TSP], BF16) for c in range(len(a_chunks))]
    W1A = [b.sb(f"W1A{c}", [128, 128], BF16) for c in range(8)]
    W1G = [b.sb(f"W1G{c}", [128, 128], BF16) for c in range(8)]
    b1a = b.sb("b1a", [128, 1], F32)
    b1g = b.sb("b1g", [128, 1], F32)

    def feature_stats(contrib, n_tok):
        for (f0, fn) in chunks(n_tok, 512):
            b.mm_group(psN[0:1, 0:fn],
                       [(ones_col[:, 0:1], contrib(c)[:, f0:f0 + fn])
                        for c in range(8)])
            b.s(lambda e, fn=fn: e.copy(SUMX[0:1, 0:fn], psN[0:1, 0:fn]))
            for c in range(8):
                b.s(lambda e, c=c, f0=f0, fn=fn: e.activation(
                    dumpA[:, 0:fn], contrib(c)[:, f0:f0 + fn], AF.Square))
                b.t(lambda e, c=c, fn=fn: e.matmul(
                    psB[0:1, 0:fn], ones_col[:, 0:1], dumpA[:, 0:fn],
                    start=(c == 0), stop=(c == 7)))
            b.s(lambda e, fn=fn: e.copy(SUMQ[0:1, 0:fn], psB[0:1, 0:fn]))
            b.s(lambda e, fn=fn: e.activation(M2[0:1, 0:fn], SUMX[0:1, 0:fn],
                                              AF.Square, scale=1.0 / D))
            b.v(lambda e, fn=fn: e.scalar_tensor_tensor(
                VRR[0:1, 0:fn], SUMQ[0:1, 0:fn], 1.0 / D, M2[0:1, 0:fn],
                ALU.mult, ALU.subtract))
            b.s(lambda e, fn=fn: e.activation(SDR[0:1, 0:fn], VRR[0:1, 0:fn],
                                              AF.Sqrt, bias=EPS))
            b.v(lambda e, fn=fn: e.reciprocal(RSR[0:1, 0:fn], SDR[0:1, 0:fn]))
            b.v(lambda e, fn=fn: e.scalar_tensor_tensor(
                MRR[0:1, 0:fn], SUMX[0:1, 0:fn], 1.0 / D, RSR[0:1, 0:fn],
                ALU.mult, ALU.mult))
            b.t(lambda e, fn=fn: e.matmul(
                psB[:, 0:fn], ones_row[0:1, :], RSR[0:1, 0:fn],
                start=True, stop=True))
            b.s(lambda e, f0=f0, fn=fn: e.copy(RSB[:, f0:f0 + fn], psB[:, 0:fn]))
            b.t(lambda e, fn=fn: e.matmul(
                psB[:, 0:fn], ones_row[0:1, :], MRR[0:1, 0:fn],
                start=True, stop=True))
            b.s(lambda e, f0=f0, fn=fn: e.copy(MRB[:, f0:f0 + fn], psB[:, 0:fn]))

    for bb in range(B_PER_CORE):
        for sp in range(NSP):
            t0 = sp * TSP
            for j in range(8):
                b.dma(XF[j][:, :], xmid[bb, j * 128:(j + 1) * 128, t0:t0 + TSP])
            for j in range(4):
                jj = j + 4
                if sp > 0:
                    b.dma(XSF[j][:, 0:TSP],
                          xmid[bb, jj * 128:(jj + 1) * 128, t0 - 1:t0 + TSP - 1])
                else:
                    b.v(lambda e, j=j: e.memset(XSF[j][:, 0:1], 0.0))
                    b.dma(XSF[j][:, 1:TSP],
                          xmid[bb, jj * 128:(jj + 1) * 128, 0:TSP - 1])

            def contrib(c):
                return XF[c][:, 0:TSP] if c < 4 else XSF[c - 4][:, 0:TSP]

            feature_stats(contrib, TSP)
            for c in range(8):
                for (f0, fn) in chunks(TSP, 512):
                    b.v(lambda e, c=c, f0=f0, fn=fn: e.tensor_mul(
                        dumpA[:, 0:fn], contrib(c)[:, f0:f0 + fn],
                        RSB[:, f0:f0 + fn]))
                    b.v(lambda e, f0=f0, fn=fn: e.tensor_sub(
                        dump2[:, 0:fn], dumpA[:, 0:fn], MRB[:, f0:f0 + fn]))
                    b.v(lambda e, c=c, f0=f0, fn=fn: e.tensor_scalar(
                        XN[c][:, f0:f0 + fn], dump2[:, 0:fn], g_col[c][:, 0:1],
                        bc_col[c][:, 0:1], ALU.mult, ALU.add))
            for ci, (a0, sz) in enumerate(a_chunks):
                b.dmac(W2R[ci][0:sz, 0:D], w2_d[a0:a0 + sz, :])
            for ci, (a0, sz) in enumerate(a_chunks):
                for ic in range(8):
                    b.dmac(W1A[ic][:, 0:sz], w1_d[ic * 128:(ic + 1) * 128,
                                                  a0:a0 + sz])
                    b.dmac(W1G[ic][:, 0:sz], w1_d[ic * 128:(ic + 1) * 128,
                                                  HID + a0:HID + a0 + sz])
                b.dma(b1a[0:sz, 0:1],
                      b1_d[a0:a0 + sz].rearrange("(p one) -> p one", one=1))
                b.dma(b1g[0:sz, 0:1],
                      b1_d[HID + a0:HID + a0 + sz].rearrange("(p one) -> p one",
                                                             one=1))
                for (f0, fn) in chunks(TSP, 512):
                    b.mm_group(psQK[0:sz, 0:fn],
                               [(W1A[c][:, 0:sz], XN[c][:, f0:f0 + fn])
                                for c in range(8)])
                    b.mm_group(psR[0:sz, 0:fn],
                               [(W1G[c][:, 0:sz], XN[c][:, f0:f0 + fn])
                                for c in range(8)])
                    b.s(lambda e, sz=sz, fn=fn: e.activation(
                        GA[0:sz, 0:fn], psR[0:sz, 0:fn], AF.Gelu,
                        bias=b1g[0:sz, 0:1]))
                    b.v(lambda e, ci=ci, sz=sz, f0=f0, fn=fn: e.scalar_tensor_tensor(
                        H2[ci][0:sz, f0:f0 + fn], psQK[0:sz, 0:fn], b1a[0:sz, 0:1],
                        GA[0:sz, 0:fn], ALU.add, ALU.mult))
            for po in range(8):
                for (f0, fn) in chunks(TSP, 512):
                    b.mm_group(psO[:, 0:fn],
                               [(W2R[ci][0:sz, po * 128:(po + 1) * 128],
                                 H2[ci][0:sz, f0:f0 + fn])
                                for ci, (a0, sz) in enumerate(a_chunks)])
                    b.v(lambda e, po=po, f0=f0, fn=fn: e.scalar_tensor_tensor(
                        XF[po][:, f0:f0 + fn], psO[:, 0:fn], b2c[po][:, 0:1],
                        XF[po][:, f0:f0 + fn], ALU.add, ALU.add))
            if cfg.postln:
                for j in range(8):
                    for (f0, fn) in chunks(TSP, 512):
                        b.dma(dumpA[:, 0:fn],
                              xblk[bb, j * 128:(j + 1) * 128, t0 + f0:t0 + f0 + fn])
                        b.v(lambda e, j=j, f0=f0, fn=fn: e.tensor_add(
                            XF[j][:, f0:f0 + fn], XF[j][:, f0:f0 + fn],
                            dumpA[:, 0:fn]))

                def contrib2(c):
                    return XF[c][:, 0:TSP]
                feature_stats(contrib2, TSP)
                for c in range(8):
                    for (f0, fn) in chunks(TSP, 512):
                        b.v(lambda e, c=c, f0=f0, fn=fn: e.tensor_mul(
                            dumpA[:, 0:fn], XF[c][:, f0:f0 + fn],
                            RSB[:, f0:f0 + fn]))
                        b.v(lambda e, f0=f0, fn=fn: e.tensor_sub(
                            dump2[:, 0:fn], dumpA[:, 0:fn], MRB[:, f0:f0 + fn]))
                        if cfg.to_token_major:
                            for k in range(fn // 128):
                                b.t(lambda e, k=k: e.transpose(
                                    psS[:, 512:640], dump2[:, k * 128:(k + 1) * 128],
                                    identf))
                                b.s(lambda e: e.copy(tout[:, :], psS[:, 512:640]))
                                b.dma(xout[bb,
                                           t0 + f0 + k * 128:t0 + f0 + (k + 1) * 128,
                                           c * 128:(c + 1) * 128],
                                      tout[:, :])
                        else:
                            b.dma(xout[bb, c * 128:(c + 1) * 128,
                                       t0 + f0:t0 + f0 + fn],
                                  dump2[:, 0:fn])
            else:
                for j in range(8):
                    b.dma(xout[bb, j * 128:(j + 1) * 128, t0:t0 + TSP], XF[j][:, :])

    ch.emit(nc, stack)
    stack.close()
    return nc


DI = 2048
DSTATE = 64
DTRANK = 64
MLPH = 4096


def _fln(b, contrib, n_tok, psN, psB, ones_col, ones_row, stats, RSB, MRB,
         dumpA, apply_fn, g_col=None, bc_col=None):
    (SUMX, SUMQ, M2R, VRR, SDR, RSR, MRR) = stats
    for (f0, fn) in chunks(n_tok, 512):
        b.mm_group(psN[0:1, 0:fn],
                   [(ones_col[:, 0:1], contrib(c)[:, f0:f0 + fn]) for c in range(8)])
        b.s(lambda e, fn=fn: e.copy(SUMX[0:1, 0:fn], psN[0:1, 0:fn]))
        for c in range(8):
            b.s(lambda e, c=c, f0=f0, fn=fn: e.activation(
                dumpA[:, 0:fn], contrib(c)[:, f0:f0 + fn], AF.Square))
            b.t(lambda e, c=c, fn=fn: e.matmul(
                psB[0:1, 0:fn], ones_col[:, 0:1], dumpA[:, 0:fn],
                start=(c == 0), stop=(c == 7)))
        b.s(lambda e, fn=fn: e.copy(SUMQ[0:1, 0:fn], psB[0:1, 0:fn]))
        b.s(lambda e, fn=fn: e.activation(M2R[0:1, 0:fn], SUMX[0:1, 0:fn],
                                          AF.Square, scale=1.0 / D))
        b.v(lambda e, fn=fn: e.scalar_tensor_tensor(
            VRR[0:1, 0:fn], SUMQ[0:1, 0:fn], 1.0 / D, M2R[0:1, 0:fn],
            ALU.mult, ALU.subtract))
        b.s(lambda e, fn=fn: e.activation(SDR[0:1, 0:fn], VRR[0:1, 0:fn],
                                          AF.Sqrt, bias=EPS))
        b.v(lambda e, fn=fn: e.reciprocal(RSR[0:1, 0:fn], SDR[0:1, 0:fn]))
        b.v(lambda e, fn=fn: e.scalar_tensor_tensor(
            MRR[0:1, 0:fn], SUMX[0:1, 0:fn], 1.0 / D, RSR[0:1, 0:fn],
            ALU.mult, ALU.mult))
        b.t(lambda e, fn=fn: e.matmul(psB[:, 0:fn], ones_row[0:1, :],
                                      RSR[0:1, 0:fn], start=True, stop=True))
        b.s(lambda e, fn=fn: e.copy(RSB[:, 0:fn], psB[:, 0:fn]))
        b.t(lambda e, fn=fn: e.matmul(psB[:, 0:fn], ones_row[0:1, :],
                                      MRR[0:1, 0:fn], start=True, stop=True))
        b.s(lambda e, fn=fn: e.copy(MRB[:, 0:fn], psB[:, 0:fn]))
        for c in range(8):
            apply_fn(c, f0, fn)


def build_transposed_weight(b, src_dram, rows, cols, dst_dram, stage_bf, psT,
                            ident, tout_bf):
    for (r0, rn) in chunks(rows, 128):
        b.dmac(stage_bf[0:rn, 0:cols], src_dram[r0:r0 + rn, 0:cols])
        for (c0, cn) in chunks(cols, 128):
            b.transpose(psT[0:cn, 0:rn], stage_bf[0:rn, c0:c0 + cn],
                        ident[0:rn, 0:rn])
            b.s(lambda e, cn=cn, rn=rn: e.copy(tout_bf[0:cn, 0:rn],
                                               psT[0:cn, 0:rn]))
            b.dma(dst_dram[c0:c0 + cn, r0:r0 + rn], tout_bf[0:cn, 0:rn])


def build_C1():
    nc = bass.Bass("TRN2", target_bir_lowering=False)
    stack = contextlib.ExitStack()
    ch = Ch()
    b = Builder(nc, stack, ch)
    b.constf(0.0, 1.0, EPS, 1.0 / D, -1.0)

    xin = nc.dram_tensor("xin", [B_PER_CORE, D, S], F32, kind="ExternalInput").ap()
    ipw = nc.dram_tensor("ipw", [2 * DI, D], F32, kind="ExternalInput").ap()
    convw = nc.dram_tensor("convw", [DI, 1, 4], F32, kind="ExternalInput").ap()
    convb = nc.dram_tensor("convb", [DI], F32, kind="ExternalInput").ap()
    xpw = nc.dram_tensor("xpw", [DTRANK + 2 * DSTATE, DI], F32,
                         kind="ExternalInput").ap()
    dtw = nc.dram_tensor("dtw", [DI, DTRANK], F32, kind="ExternalInput").ap()
    dtb = nc.dram_tensor("dtb", [DI], F32, kind="ExternalInput").ap()
    ln1w = nc.dram_tensor("ln1w", [D], F32, kind="ExternalInput").ap()
    ln1b = nc.dram_tensor("ln1b", [D], F32, kind="ExternalInput").ap()
    identd = nc.dram_tensor("ident", [128, 128], BF16, kind="ExternalInput").ap()
    delt_o = nc.dram_tensor("delt", [B_PER_CORE, DI, S], F32,
                            kind="ExternalOutput").ap()
    du_o = nc.dram_tensor("du", [B_PER_CORE, DI, S], BF16,
                          kind="ExternalOutput").ap()
    u_o = nc.dram_tensor("u", [B_PER_CORE, DI, S], BF16,
                         kind="ExternalOutput").ap()
    zs_o = nc.dram_tensor("zs", [B_PER_CORE, DI, S], BF16,
                          kind="ExternalOutput").ap()
    xdb_o = nc.dram_tensor("xdb", [B_PER_CORE, 128, S], F32,
                           kind="ExternalOutput").ap()
    xdc_o = nc.dram_tensor("xdc", [B_PER_CORE, 64, S], F32,
                           kind="ExternalOutput").ap()
    ipT = nc.dram_tensor("ipT", [D, 2 * DI], BF16, kind="Internal").ap()
    xpT = nc.dram_tensor("xpT", [DI, DTRANK + 2 * DSTATE], BF16,
                         kind="Internal").ap()
    dtT = nc.dram_tensor("dtT", [DTRANK, DI], BF16, kind="Internal").ap()

    ident_t = b.sb("ident", [128, 128], BF16)
    b.dma(ident_t[:, :], identd)
    ident = ident_t[:, :]
    ones_col = b.sb("ones_col", [128, 1], F32)
    b.v(lambda e: e.memset(ones_col[:, 0:1], 1.0))
    ones_row = b.sb("ones_row", [1, 128], F32)
    b.v(lambda e: e.memset(ones_row[0:1, :], 1.0))
    psT = b.ps("psT", [128, 128], BF16)
    psQ = b.ps("psQ", [128, 512], F32)
    psN = b.ps("psN", [2, 512], F32)
    psB = b.ps("psB", [128, 512], F32)
    stage_bf = b.sb("stage_bf", [128, DI], BF16)
    tout_bf = b.sb("tout_bf", [128, 128], BF16)

    build_transposed_weight(b, ipw, 2 * DI, D, ipT, stage_bf, psT, ident, tout_bf)
    build_transposed_weight(b, xpw, DTRANK + 2 * DSTATE, DI, xpT, stage_bf, psT,
                            ident, tout_bf)
    build_transposed_weight(b, dtw, DI, DTRANK, dtT, stage_bf, psT, ident, tout_bf)

    CW = [b.sb(f"CW{k}", [128, 4], F32) for k in range(16)]
    CB = [b.sb(f"CBc{k}", [128, 1], F32) for k in range(16)]
    DTBc = [b.sb(f"DTB{k}", [128, 1], F32) for k in range(16)]
    for k in range(16):
        b.dma(CW[k][:, 0:4], convw[k * 128:(k + 1) * 128, 0, :])
        b.col_from_dram(convb, k * 128, 128, CB[k])
        b.col_from_dram(dtb, k * 128, 128, DTBc[k])
    g_col = [b.sb(f"gcol{j}", [128, 1], F32) for j in range(8)]
    bc_col = [b.sb(f"bcol{j}", [128, 1], F32) for j in range(8)]
    for j in range(8):
        b.col_from_dram(ln1w, j * 128, 128, g_col[j])
        b.col_from_dram(ln1b, j * 128, 128, bc_col[j])

    XN = [b.sb(f"XN{j}", [128, S], BF16) for j in range(8)]
    stats = tuple(b.sb(f"st{i}", [1, 512], F32) for i in range(7))
    RSB = b.sb("RSB", [128, 512], F32)
    MRB = b.sb("MRB", [128, 512], F32)
    dumpA = b.sb("dumpA", [128, 512], F32)
    dump2 = b.sb("dump2", [128, 512], F32)

    TC = 512
    ST16 = [b.sb(f"ST16_{c}", [128, 128], BF16) for c in range(16)]

    for bb in range(B_PER_CORE):
        xstack = contextlib.ExitStack()
        XF = [b.sb_in(xstack, f"XF{j}", [128, S], F32) for j in range(8)]
        for j in range(8):
            b.dma(XF[j][:, :], xin[bb, j * 128:(j + 1) * 128, :])

        def contrib(c, XF=XF):
            return XF[c][:, 0:S]

        def apply(c, f0, fn, XF=XF):
            b.v(lambda e, c=c, f0=f0, fn=fn: e.tensor_mul(
                dumpA[:, 0:fn], contrib(c)[:, f0:f0 + fn], RSB[:, 0:fn]))
            b.v(lambda e, fn=fn: e.tensor_sub(dump2[:, 0:fn], dumpA[:, 0:fn],
                                              MRB[:, 0:fn]))
            b.v(lambda e, c=c, f0=f0, fn=fn: e.tensor_scalar(
                XN[c][:, f0:f0 + fn], dump2[:, 0:fn], g_col[c][:, 0:1],
                bc_col[c][:, 0:1], ALU.mult, ALU.add))

        _fln(b, contrib, S, psN, psB, ones_col, ones_row, stats, RSB, MRB,
             dumpA, apply)
        xstack.close()
        sstack = contextlib.ExitStack()
        XIR = [b.sb_in(sstack, f"XIR{k}", [128, TC + 3], F32) for k in range(16)]
        U = [b.sb_in(sstack, f"U{k}", [128, TC], F32) for k in range(16)]
        UB = [b.sb_in(sstack, f"UB{k}", [128, TC], BF16) for k in range(16)]
        U2 = b.sb_in(sstack, "U2", [128, TC], F32)
        XDB = b.sb_in(sstack, "XDB", [128, TC], F32)
        XDC = b.sb_in(sstack, "XDC", [128, TC], F32)
        DTB16 = b.sb_in(sstack, "DTB16", [64, TC], BF16)
        E1 = b.sb_in(sstack, "E1", [128, TC], F32)
        E2 = b.sb_in(sstack, "E2", [128, TC], F32)
        DEL = b.sb_in(sstack, "DEL", [128, TC], F32)
        UB2 = b.sb_in(sstack, "UB2", [128, TC], BF16)
        ZSB = b.sb_in(sstack, "ZSB", [128, TC], BF16)

        for sp in range(S // TC):
            t0 = sp * TC
            for k in range(16):
                if sp == 0:
                    b.v(lambda e, k=k: e.memset(XIR[k][:, 0:3], 0.0))
                else:
                    b.v(lambda e, k=k: e.tensor_copy(XIR[k][:, 0:3],
                                                     XIR[k][:, TC:TC + 3]))
            for po in range(32):
                for c in range(8):
                    b.dma(ST16[c][:, :],
                          ipT[c * 128:(c + 1) * 128, po * 128:(po + 1) * 128])
                b.mm_group(psQ[:, 0:TC],
                           [(ST16[c][:, :], XN[c][:, t0:t0 + TC]) for c in range(8)])
                if po < 16:
                    b.s(lambda e, po=po: e.copy(XIR[po][:, 3:3 + TC], psQ[:, 0:TC]))
                else:
                    b.s(lambda e: e.activation(ZSB[:, 0:TC], psQ[:, 0:TC],
                                               AF.Silu))
                    b.dma(zs_o[bb, (po - 16) * 128:(po - 15) * 128, t0:t0 + TC],
                          ZSB[:, 0:TC])
            for k in range(16):
                b.v(lambda e, k=k: e.tensor_scalar(
                    U[k][:, 0:TC], XIR[k][:, 0:TC], CW[k][:, 0:1], None, ALU.mult))
                for tap in range(1, 4):
                    src, dst = (U[k], U2) if tap % 2 == 1 else (U2, U[k])
                    b.v(lambda e, k=k, tap=tap, src=src, dst=dst:
                        e.scalar_tensor_tensor(
                            dst[:, 0:TC], XIR[k][:, tap:tap + TC],
                            CW[k][:, tap:tap + 1], src[:, 0:TC],
                            ALU.mult, ALU.add))
                b.s(lambda e, k=k: e.activation(U[k][:, 0:TC], U2[:, 0:TC],
                                                AF.Silu, bias=CB[k][:, 0:1]))
                b.s(lambda e, k=k: e.copy(UB[k][:, 0:TC], U[k][:, 0:TC]))
                b.dma(u_o[bb, k * 128:(k + 1) * 128, t0:t0 + TC], UB[k][:, 0:TC])
            for po, (p0, pn) in enumerate(chunks(DTRANK + 2 * DSTATE, 128)):
                for c in range(16):
                    b.dma(ST16[c][:, 0:pn], xpT[c * 128:(c + 1) * 128, p0:p0 + pn])
                b.mm_group(psQ[0:pn, 0:TC],
                           [(ST16[c][:, 0:pn], UB[c][:, 0:TC]) for c in range(16)])
                dst = XDB if po == 0 else XDC
                b.s(lambda e, dst=dst, pn=pn: e.copy(dst[0:pn, 0:TC],
                                                     psQ[0:pn, 0:TC]))
            b.dma(xdb_o[bb, :, t0:t0 + TC], XDB[:, 0:TC])
            b.dma(xdc_o[bb, :, t0:t0 + TC], XDC[0:64, 0:TC])
            b.s(lambda e: e.copy(DTB16[0:64, 0:TC], XDB[0:64, 0:TC]))
            for k in range(16):
                b.dma(ST16[0][0:64, 0:128], dtT[0:64, k * 128:(k + 1) * 128])
                b.t(lambda e: e.matmul(psQ[:, 0:TC], ST16[0][0:64, 0:128],
                                       DTB16[0:64, 0:TC], start=True, stop=True))
                b.s(lambda e, k=k: e.activation(E1[:, 0:TC], psQ[:, 0:TC], AF.Abs,
                                                bias=DTBc[k][:, 0:1]))
                b.s(lambda e: e.activation(E2[:, 0:TC], E1[:, 0:TC], AF.Exp,
                                           scale=-1.0))
                b.s(lambda e: e.activation(E1[:, 0:TC], E2[:, 0:TC], AF.Ln,
                                           bias=1.0))
                b.s(lambda e, k=k: e.activation(DEL[:, 0:TC], psQ[:, 0:TC], AF.Relu,
                                                bias=DTBc[k][:, 0:1]))
                b.v(lambda e: e.tensor_add(DEL[:, 0:TC], DEL[:, 0:TC], E1[:, 0:TC]))
                b.dma(delt_o[bb, k * 128:(k + 1) * 128, t0:t0 + TC], DEL[:, 0:TC])
                b.v(lambda e, k=k: e.tensor_mul(dumpA[:, 0:TC], DEL[:, 0:TC],
                                                U[k][:, 0:TC]))
                b.s(lambda e, k=k: e.copy(UB2[:, 0:TC], dumpA[:, 0:TC]))
                b.dma(du_o[bb, k * 128:(k + 1) * 128, t0:t0 + TC], UB2[:, 0:TC])
        sstack.close()

    ch.emit(nc, stack)
    stack.close()
    return nc


def build_C2():
    nc = bass.Bass("TRN2", target_bir_lowering=False)
    stack = contextlib.ExitStack()
    ch = Ch()
    b = Builder(nc, stack, ch)
    b.constf(0.0, 1.0, -1.0)

    delt = nc.dram_tensor("delt", [B_PER_CORE, DI, S], F32,
                          kind="ExternalInput").ap()
    du = nc.dram_tensor("du", [B_PER_CORE, DI, S], BF16,
                        kind="ExternalInput").ap()
    xdb = nc.dram_tensor("xdb", [B_PER_CORE, 128, S], F32,
                         kind="ExternalInput").ap()
    xdc = nc.dram_tensor("xdc", [B_PER_CORE, 64, S], F32,
                         kind="ExternalInput").ap()
    alog = nc.dram_tensor("alog", [DI, DSTATE], F32, kind="ExternalInput").ap()
    y_o = nc.dram_tensor("y", [B_PER_CORE, DI, S], F32,
                         kind="ExternalOutput").ap()

    psB = b.ps("psB", [128, 512], F32)
    ones_row = b.sb("ones_row", [1, 128], F32)
    b.v(lambda e: e.memset(ones_row[0:1, :], 1.0))

    ATL = [b.sb(f"ATL{k}", [128, DSTATE], F32) for k in range(16)]
    ATLt = b.sb("ATLt", [128, DSTATE], F32)
    for k in range(16):
        b.dma(ATLt[:, :], alog[k * 128:(k + 1) * 128, :])
        b.s(lambda e, k=k: e.activation(ATL[k][:, :], ATLt[:, :], AF.Exp))
        b.v(lambda e, k=k: e.tensor_scalar(ATL[k][:, :], ATL[k][:, :], -1.0,
                                           None, ALU.mult))
    HC = [b.sb(f"HC{k}", [128, DSTATE], F32) for k in range(16)]

    TH = 1024
    DELT = [b.sb(f"DELT{k}", [128, TH], F32) for k in range(16)]
    DUB = [b.sb(f"DUB{k}", [128, TH], BF16) for k in range(16)]
    YACC = [b.sb(f"YACC{k}", [128, TH], F32) for k in range(16)]
    Bb = b.sb("Bb", [128, TH], F32)
    Cb = b.sb("Cb", [128, TH], F32)
    AE = b.sb("AE", [128, TH], F32)
    BF = b.sb("BF", [128, TH], F32)
    H = b.sb("H", [128, TH], F32)
    BR = b.sb("BR", [1, 512], F32)

    for bb in range(B_PER_CORE):
        for half in range(2):
            t0 = half * TH
            for k in range(16):
                b.dma(DELT[k][:, :], delt[bb, k * 128:(k + 1) * 128, t0:t0 + TH])
                b.dma(DUB[k][:, :], du[bb, k * 128:(k + 1) * 128, t0:t0 + TH])
                b.v(lambda e, k=k: e.memset(YACC[k][:, :], 0.0))
            for n in range(DSTATE):
                for (f0, fn) in chunks(TH, 512):
                    b.dma(BR[0:1, 0:fn], xdb[bb, 64 + n:65 + n, t0 + f0:t0 + f0 + fn])
                    b.t(lambda e, fn=fn: e.matmul(psB[:, 0:fn], ones_row[0:1, :],
                                                  BR[0:1, 0:fn], start=True,
                                                  stop=True))
                    b.s(lambda e, f0=f0, fn=fn: e.copy(Bb[:, f0:f0 + fn],
                                                       psB[:, 0:fn]))
                    b.dma(BR[0:1, 0:fn], xdc[bb, n:n + 1, t0 + f0:t0 + f0 + fn])
                    b.t(lambda e, fn=fn: e.matmul(psB[:, 0:fn], ones_row[0:1, :],
                                                  BR[0:1, 0:fn], start=True,
                                                  stop=True))
                    b.s(lambda e, f0=f0, fn=fn: e.copy(Cb[:, f0:f0 + fn],
                                                       psB[:, 0:fn]))
                for k in range(16):
                    b.s(lambda e, k=k, n=n: e.activation(
                        AE[:, :], DELT[k][:, :], AF.Exp,
                        scale=ATL[k][:, n:n + 1]))
                    b.v(lambda e, k=k: e.tensor_mul(BF[:, :], DUB[k][:, :],
                                                    Bb[:, :]))
                    init = 0.0 if half == 0 else HC[k][:, 0:DSTATE][:, 0:1]
                    if half == 1:
                        init = HC[k][:, n:n + 1]
                    b.v(lambda e, init=init: e.tensor_tensor_scan(
                        H[:, :], AE[:, :], BF[:, :], init, ALU.mult, ALU.add))
                    if half == 0:
                        b.v(lambda e, k=k, n=n: e.tensor_copy(
                            HC[k][:, n:n + 1], H[:, TH - 1:TH]))
                    b.v(lambda e: e.tensor_mul(AE[:, :], H[:, :], Cb[:, :]))
                    b.v(lambda e, k=k: e.tensor_add(YACC[k][:, :], YACC[k][:, :],
                                                    AE[:, :]))
            for k in range(16):
                b.dma(y_o[bb, k * 128:(k + 1) * 128, t0:t0 + TH], YACC[k][:, :])

    ch.emit(nc, stack)
    stack.close()
    return nc


def build_C3():
    nc = bass.Bass("TRN2", target_bir_lowering=False)
    stack = contextlib.ExitStack()
    astack = contextlib.ExitStack()
    ch = Ch()
    b = Builder(nc, stack, ch)
    b.constf(0.0, 1.0, EPS, 1.0 / D)

    xin = nc.dram_tensor("xin", [B_PER_CORE, D, S], F32, kind="ExternalInput").ap()
    y_d = nc.dram_tensor("y", [B_PER_CORE, DI, S], F32, kind="ExternalInput").ap()
    u_d = nc.dram_tensor("u", [B_PER_CORE, DI, S], BF16, kind="ExternalInput").ap()
    zs_d = nc.dram_tensor("zs", [B_PER_CORE, DI, S], BF16,
                          kind="ExternalInput").ap()
    dp_d = nc.dram_tensor("dp", [DI], F32, kind="ExternalInput").ap()
    opw = nc.dram_tensor("opw", [D, DI], F32, kind="ExternalInput").ap()
    m1w = nc.dram_tensor("m1w", [MLPH, D], F32, kind="ExternalInput").ap()
    m1b = nc.dram_tensor("m1b", [MLPH], F32, kind="ExternalInput").ap()
    m2w = nc.dram_tensor("m2w", [D, MLPH], F32, kind="ExternalInput").ap()
    m2b = nc.dram_tensor("m2b", [D], F32, kind="ExternalInput").ap()
    ln4w = nc.dram_tensor("ln4w", [D], F32, kind="ExternalInput").ap()
    ln4b = nc.dram_tensor("ln4b", [D], F32, kind="ExternalInput").ap()
    ln5w = nc.dram_tensor("ln5w", [D], F32, kind="ExternalInput").ap()
    ln5b = nc.dram_tensor("ln5b", [D], F32, kind="ExternalInput").ap()
    ln6w = nc.dram_tensor("ln6w", [D], F32, kind="ExternalInput").ap()
    ln6b = nc.dram_tensor("ln6b", [D], F32, kind="ExternalInput").ap()
    identd = nc.dram_tensor("ident", [128, 128], BF16, kind="ExternalInput").ap()
    xout = nc.dram_tensor("xout", [B_PER_CORE, D, S], F32,
                          kind="ExternalOutput").ap()
    opT = nc.dram_tensor("opT", [DI, D], BF16, kind="Internal").ap()
    m1T = nc.dram_tensor("m1T", [D, MLPH], BF16, kind="Internal").ap()
    m2T = nc.dram_tensor("m2T", [MLPH, D], BF16, kind="Internal").ap()

    ident_t = b.sb("ident", [128, 128], BF16)
    b.dma(ident_t[:, :], identd)
    ident = ident_t[:, :]
    ones_col = b.sb("ones_col", [128, 1], F32)
    b.v(lambda e: e.memset(ones_col[:, 0:1], 1.0))
    ones_row = b.sb("ones_row", [1, 128], F32)
    b.v(lambda e: e.memset(ones_row[0:1, :], 1.0))
    psT = b.ps("psT", [128, 128], BF16)
    psQ = b.ps("psQ", [128, 512], F32)
    psN = b.ps("psN", [2, 512], F32)
    psB = b.ps("psB", [128, 512], F32)
    psG = b.ps("psG", [128, 512], F32)
    wstack = contextlib.ExitStack()
    stage_bf = b.sb_in(wstack, "stage_bf", [128, MLPH], BF16)
    tout_bf = b.sb_in(wstack, "tout_bf", [128, 128], BF16)
    build_transposed_weight(b, opw, D, DI, opT, stage_bf, psT, ident, tout_bf)
    build_transposed_weight(b, m1w, MLPH, D, m1T, stage_bf, psT, ident, tout_bf)
    build_transposed_weight(b, m2w, D, MLPH, m2T, stage_bf, psT, ident, tout_bf)
    wstack.close()

    DPc = [b.sb(f"DPc{k}", [128, 1], F32) for k in range(16)]
    for k in range(16):
        b.col_from_dram(dp_d, k * 128, 128, DPc[k])
    l4g = [b.sb(f"l4g{j}", [128, 1], F32) for j in range(8)]
    l4b = [b.sb(f"l4b{j}", [128, 1], F32) for j in range(8)]
    l5g = [b.sb(f"l5g{j}", [128, 1], F32) for j in range(8)]
    l5b = [b.sb(f"l5b{j}", [128, 1], F32) for j in range(8)]
    l6g = [b.sb(f"l6g{j}", [128, 1], F32) for j in range(8)]
    l6b = [b.sb(f"l6b{j}", [128, 1], F32) for j in range(8)]
    for j in range(8):
        b.col_from_dram(ln4w, j * 128, 128, l4g[j])
        b.col_from_dram(ln4b, j * 128, 128, l4b[j])
        b.col_from_dram(ln5w, j * 128, 128, l5g[j])
        b.col_from_dram(ln5b, j * 128, 128, l5b[j])
        b.col_from_dram(ln6w, j * 128, 128, l6g[j])
        b.col_from_dram(ln6b, j * 128, 128, l6b[j])
    m1bc = [b.sb(f"m1bc{j}", [128, 1], F32) for j in range(32)]
    for j in range(32):
        b.col_from_dram(m1b, j * 128, 128, m1bc[j])
    m2bc = [b.sb(f"m2bc{j}", [128, 1], F32) for j in range(8)]
    for j in range(8):
        b.col_from_dram(m2b, j * 128, 128, m2bc[j])

    XF = [b.sb(f"XF{j}", [128, S], F32) for j in range(8)]
    XN = [b.sb(f"XN{j}", [128, S], BF16) for j in range(8)]
    stats = tuple(b.sb(f"st{i}", [1, 512], F32) for i in range(7))
    RSB = b.sb("RSB", [128, 512], F32)
    MRB = b.sb("MRB", [128, 512], F32)
    dumpA = b.sb("dumpA", [128, 512], F32)
    dump2 = b.sb("dump2", [128, 512], F32)
    ST16 = [b.sb(f"ST16_{c}", [128, 128], BF16) for c in range(32)]
    dumpF = b.sb("dumpF", [128, 1024], F32)

    def fln_apply_inplace(tiles, gc, bc, out_bf=None, out_writer=None):
        def apply(c, f0, fn):
            b.v(lambda e, c=c, f0=f0, fn=fn: e.tensor_mul(
                dumpA[:, 0:fn], tiles[c][:, f0:f0 + fn], RSB[:, 0:fn]))
            b.v(lambda e, fn=fn: e.tensor_sub(dump2[:, 0:fn], dumpA[:, 0:fn],
                                              MRB[:, 0:fn]))
            if out_writer is not None:
                out_writer(c, f0, fn)
            elif out_bf is not None:
                b.v(lambda e, c=c, f0=f0, fn=fn: e.tensor_scalar(
                    out_bf[c][:, f0:f0 + fn], dump2[:, 0:fn], gc[c][:, 0:1],
                    bc[c][:, 0:1], ALU.mult, ALU.add))
            else:
                b.v(lambda e, c=c, f0=f0, fn=fn: e.tensor_scalar(
                    tiles[c][:, f0:f0 + fn], dump2[:, 0:fn], gc[c][:, 0:1],
                    bc[c][:, 0:1], ALU.mult, ALU.add))
        return apply

    for bb in range(B_PER_CORE):
        bstack = contextlib.ExitStack()
        YZB = [b.sb_in(bstack, f"YZB{k}", [128, S], BF16) for k in range(16)]
        YK = b.sb_in(bstack, "YK", [128, 1024], F32)
        UK = b.sb_in(bstack, "UK", [128, 1024], BF16)
        ZK = b.sb_in(bstack, "ZK", [128, 1024], BF16)
        for k in range(16):
            for hh in range(2):
                h0 = hh * 1024
                b.dma(YK[:, :], y_d[bb, k * 128:(k + 1) * 128, h0:h0 + 1024])
                b.dma(UK[:, :], u_d[bb, k * 128:(k + 1) * 128, h0:h0 + 1024])
                b.dma(ZK[:, :], zs_d[bb, k * 128:(k + 1) * 128, h0:h0 + 1024])
                b.v(lambda e, k=k: e.scalar_tensor_tensor(
                    dumpF[:, 0:1024], UK[:, 0:1024], DPc[k][:, 0:1],
                    YK[:, 0:1024], ALU.mult, ALU.add))
                b.v(lambda e, k=k, h0=h0: e.tensor_mul(
                    YZB[k][:, h0:h0 + 1024], dumpF[:, 0:1024], ZK[:, 0:1024]))
        for j in range(8):
            b.dma(XF[j][:, :], xin[bb, j * 128:(j + 1) * 128, :])
        for po in range(8):
            for c in range(16):
                b.dma(ST16[c][:, :], opT[c * 128:(c + 1) * 128,
                                         po * 128:(po + 1) * 128])
            for (f0, fn) in chunks(S, 512):
                b.mm_group(psQ[:, 0:fn],
                           [(ST16[c][:, :], YZB[c][:, f0:f0 + fn])
                            for c in range(16)])
                b.v(lambda e, po=po, f0=f0, fn=fn: e.tensor_add(
                    XF[po][:, f0:f0 + fn], psQ[:, 0:fn], XF[po][:, f0:f0 + fn]))
        bstack.close()
        _fln(b, lambda c: XF[c][:, 0:S], S, psN, psB, ones_col, ones_row, stats,
             RSB, MRB, dumpA, fln_apply_inplace(XF, l4g, l4b))
        _fln(b, lambda c: XF[c][:, 0:S], S, psN, psB, ones_col, ones_row, stats,
             RSB, MRB, dumpA, fln_apply_inplace(XF, l5g, l5b, out_bf=XN))
        mstack = contextlib.ExitStack()
        H1 = [b.sb_in(mstack, f"H1_{j}", [128, 1024], BF16) for j in range(32)]
        for half in range(2):
            t0 = half * 1024
            for po in range(32):
                for c in range(8):
                    b.dma(ST16[c][:, :], m1T[c * 128:(c + 1) * 128,
                                             po * 128:(po + 1) * 128])
                for (f0, fn) in chunks(1024, 512):
                    b.mm_group(psQ[:, 0:fn],
                               [(ST16[c][:, :], XN[c][:, t0 + f0:t0 + f0 + fn])
                                for c in range(8)])
                    b.s(lambda e, po=po, f0=f0, fn=fn: e.activation(
                        H1[po][:, f0:f0 + fn], psQ[:, 0:fn], AF.Gelu,
                        bias=m1bc[po][:, 0:1]))
            for po in range(8):
                for c in range(32):
                    b.dma(ST16[c][:, :], m2T[c * 128:(c + 1) * 128,
                                             po * 128:(po + 1) * 128])
                for (f0, fn) in chunks(1024, 512):
                    b.mm_group(psG[:, 0:fn],
                               [(ST16[c][:, :], H1[c][:, f0:f0 + fn])
                                for c in range(32)])
                    b.v(lambda e, po=po, f0=f0, fn=fn, t0=t0:
                        e.scalar_tensor_tensor(
                            XF[po][:, t0 + f0:t0 + f0 + fn], psG[:, 0:fn],
                            m2bc[po][:, 0:1], XF[po][:, t0 + f0:t0 + f0 + fn],
                            ALU.add, ALU.add))
        mstack.close()
        _fln(b, lambda c: XF[c][:, 0:S], S, psN, psB, ones_col, ones_row, stats,
             RSB, MRB, dumpA, fln_apply_inplace(XF, l6g, l6b))
        for j in range(8):
            for hh in range(2):
                h0 = hh * 1024
                b.dma(dumpF[:, 0:1024], xin[bb, j * 128:(j + 1) * 128,
                                            h0:h0 + 1024])
                b.v(lambda e, j=j, h0=h0: e.tensor_add(
                    XF[j][:, h0:h0 + 1024], XF[j][:, h0:h0 + 1024],
                    dumpF[:, 0:1024]))

        def plain_writer(c, f0, fn):
            b.dma(xout[bb, c * 128:(c + 1) * 128, f0:f0 + fn], dump2[:, 0:fn])

        _fln(b, lambda c: XF[c][:, 0:S], S, psN, psB, ones_col, ones_row, stats,
             RSB, MRB, dumpA, fln_apply_inplace(XF, None, None,
                                                out_writer=plain_writer))

    ch.emit(nc, stack)
    stack.close()
    return nc


def make_rotary_tables(window, n=S):
    half = np.arange(0, 64, 2, dtype=np.float64)
    inv_freq = 1.0 / 10000.0 ** (half / 64.0)
    t = np.arange(n, dtype=np.float64)
    freqs = t[:, None] * inv_freq[None, :]
    freqs = np.concatenate([freqs, freqs], -1)
    base = (half + 0.4 * 64) / (1.4 * 64)
    power = (t - n // 2) / (window // 2)
    scale = base[None, :] ** power[:, None]
    scale = np.concatenate([scale, scale], -1)
    cs = np.cos(freqs) * scale
    ss = np.sin(freqs) * scale
    csk = np.cos(freqs) / scale
    ssk = np.sin(freqs) / scale

    def fm(a):
        aT = a.T.astype(np.float32)
        return np.concatenate([aT, aT], axis=0)

    return fm(cs), fm(ss), fm(csk), fm(ssk)


def make_masks(window):
    i = np.arange(window)[:, None]
    j1 = np.arange(2 * window)[None, :]
    m1 = np.where((j1 > i + window) | (j1 < i), -1e30, 0.0).astype(np.float32)
    j0 = np.arange(window)[None, :]
    m0 = np.where(j0 > i, -1e30, 0.0).astype(np.float32)
    return m1, m0


def _sel64():
    a = np.zeros((128, 2), dtype=np.float32)
    a[0:64, 0] = 1.0
    a[64:128, 1] = 1.0
    return a


def _selt():
    a = np.zeros((2, 128), dtype=np.float32)
    a[0, 0:64] = 1.0
    a[1, 64:128] = 1.0
    return a


def make_rm():
    rm = np.zeros((128, 128), dtype=np.float32)
    for blk in (0, 64):
        for r in range(32):
            rm[blk + r + 32, blk + r] = -1.0
            rm[blk + r, blk + r + 32] = 1.0
    return rm


def shard_inputs(x_full):
    return [np.ascontiguousarray(x_full[i * B_PER_CORE:(i + 1) * B_PER_CORE])
            for i in range(NCORES)]


def _ident_bf16():
    return np.eye(128, dtype=ml_dtypes.bfloat16)


def _ident_f32():
    return np.eye(128, dtype=np.float32)


def _f32(a):
    return np.ascontiguousarray(np.asarray(a), dtype=np.float32)


def run_A(x_shards, inputs):
    nc = build_A()
    shared = {
        "wq": _f32(inputs["L0_attn_Wq"]),
        "bq": _f32(inputs["L0_attn_bq"]),
        "wv": _f32(inputs["L0_attn_Wv"]),
        "bv": _f32(inputs["L0_attn_bv"]),
        "anom": _f32(inputs["L0_attn_anomaly_matrix"]),
        "ident": _ident_bf16(),
        "identf": _ident_f32(),
    }
    in_maps = [dict(shared, xin=s) for s in x_shards]
    res = run_bass_kernel_spmd(nc, in_maps, core_ids=list(range(NCORES)))
    return [r["xout"] for r in res.results]


def layer_maps(cfg, weights, x_shards):
    csq, ssq, csk, ssk = make_rotary_tables(cfg.W)
    m1, m0 = make_masks(cfg.W)
    shared = dict(weights)
    shared.update({
        "ident": _ident_bf16(), "identf": _ident_f32(), "rm": make_rm(),
        "csq": csq, "ssq": ssq, "csk": csk, "ssk": ssk,
        "m1": m1, "m0": m0,
        "sel64": _sel64(), "selt": _selt(),
    })
    return [dict(shared, xin=s) for s in x_shards]


def run_layers(cfg, nc, per_layer_weights, x_shards, blk_shards=None):
    for weights in per_layer_weights:
        in_maps = layer_maps(cfg, weights, x_shards)
        if cfg.postln:
            for i in range(len(in_maps)):
                in_maps[i]["xblk"] = blk_shards[i]
        res = run_bass_kernel_spmd(nc, in_maps, core_ids=list(range(NCORES)))
        x_shards = [r["xout"] for r in res.results]
    return x_shards


def l0_ffn_weights(inputs, i):
    return {
        "wqkv": _f32(inputs["L0_ffn_Wqkv"][i]),
        "qs": _f32(inputs["L0_ffn_q_scale"][i]),
        "ks": _f32(inputs["L0_ffn_k_scale"][i]),
        "wo": _f32(inputs["L0_ffn_Wo"][i]),
        "lng": _f32(inputs["L0_ffn_ln_g"][i]),
        "lnb": _f32(inputs["L0_ffn_ln_b"][i]),
        "w1": _f32(inputs["L0_ffn_W1"][i]),
        "b1": _f32(inputs["L0_ffn_b1"][i]),
        "w2": _f32(inputs["L0_ffn_W2"][i]),
        "b2": _f32(inputs["L0_ffn_b2"][i]),
    }


def l1_ffn_weights(inputs, i):
    return {
        "wqkv": _f32(inputs["L1_ffn_wqkv"][i]),
        "qs": _f32(inputs["L1_ffn_q_scale"][i]),
        "ks": _f32(inputs["L1_ffn_k_scale"][i]),
        "wo": _f32(inputs["L1_ffn_wo"][i]),
        "lng": _f32(inputs["L1_ffn_ln_g"][i]),
        "lnb": _f32(inputs["L1_ffn_ln_b"][i]),
        "w1": _f32(inputs["L1_ffn_w1"][i]),
        "b1": _f32(inputs["L1_ffn_b1"][i]),
        "w2": _f32(inputs["L1_ffn_w2"][i]),
        "b2": _f32(inputs["L1_ffn_b2"][i]),
    }


def run_C(inputs, x_shards):
    cores = list(range(NCORES))
    nc1 = build_C1()
    sh1 = {
        "ipw": _f32(inputs["L1_attn_in_proj_w"]),
        "convw": _f32(inputs["L1_attn_conv_w"]),
        "convb": _f32(inputs["L1_attn_conv_b"]),
        "xpw": _f32(inputs["L1_attn_x_proj_w"]),
        "dtw": _f32(inputs["L1_attn_dt_proj_w"]),
        "dtb": _f32(inputs["L1_attn_dt_proj_b"]),
        "ln1w": _f32(inputs["L1_attn_ln1_w"]),
        "ln1b": _f32(inputs["L1_attn_ln1_b"]),
        "ident": _ident_bf16(),
    }
    res1 = run_bass_kernel_spmd(nc1, [dict(sh1, xin=s) for s in x_shards],
                                core_ids=cores)
    nc2 = build_C2()
    sh2 = {"alog": _f32(inputs["L1_attn_A_log"])}
    maps2 = [dict(sh2, delt=r["delt"], du=r["du"], xdb=r["xdb"], xdc=r["xdc"])
             for r in res1.results]
    res2 = run_bass_kernel_spmd(nc2, maps2, core_ids=cores)
    nc3 = build_C3()
    sh3 = {
        "dp": _f32(inputs["L1_attn_Dp"]),
        "opw": _f32(inputs["L1_attn_out_proj_w"]),
        "m1w": _f32(inputs["L1_attn_mlp_w1"]),
        "m1b": _f32(inputs["L1_attn_mlp_b1"]),
        "m2w": _f32(inputs["L1_attn_mlp_w2"]),
        "m2b": _f32(inputs["L1_attn_mlp_b2"]),
        "ln4w": _f32(inputs["L1_attn_ln4_w"]),
        "ln4b": _f32(inputs["L1_attn_ln4_b"]),
        "ln5w": _f32(inputs["L1_attn_ln5_w"]),
        "ln5b": _f32(inputs["L1_attn_ln5_b"]),
        "ln6w": _f32(inputs["L1_attn_ln6_w"]),
        "ln6b": _f32(inputs["L1_attn_ln6_b"]),
        "ident": _ident_bf16(),
    }
    maps3 = [dict(sh3, xin=x_shards[i], y=res2.results[i]["y"],
                  u=res1.results[i]["u"], zs=res1.results[i]["zs"])
             for i in range(NCORES)]
    res3 = run_bass_kernel_spmd(nc3, maps3, core_ids=cores)
    return [r["xout"] for r in res3.results]


def kernel(**inputs):
    x = np.asarray(inputs["L0_attn_hidden_states"], dtype=np.float32)
    shards = shard_inputs(x)
    shards = run_A(shards, inputs)
    blk_in = shards
    cfgB = LayerCfg(16, 256, False)
    ncB = build_layer(cfgB)
    shards = run_layers(cfgB, ncB, [l0_ffn_weights(inputs, i) for i in range(5)],
                        shards)
    cfgB2 = LayerCfg(16, 256, True)
    ncB2 = build_layer(cfgB2)
    shards = run_layers(cfgB2, ncB2, [l0_ffn_weights(inputs, 5)], shards,
                        blk_shards=blk_in)
    shards = run_C(inputs, shards)
    blk_in = shards
    cfgD = LayerCfg(8, 512, False)
    ncD = build_layer(cfgD)
    shards = run_layers(cfgD, ncD, [l1_ffn_weights(inputs, 0)], shards)
    cfgD2 = LayerCfg(8, 512, True)
    cfgD2.to_token_major = True
    ncD2 = build_layer(cfgD2)
    shards = run_layers(cfgD2, ncD2, [l1_ffn_weights(inputs, 1)], shards,
                        blk_shards=blk_in)
    return np.concatenate(shards, axis=0)
